# version 16 (final)
# speedup vs baseline: 1.1398x; 1.0227x over previous
import numpy as np
import ml_dtypes
from contextlib import ExitStack

import concourse.bass as bass
import concourse.mybir as mybir
from concourse.bass_utils import run_bass_kernel_spmd

F32 = mybir.dt.float32
BF16 = mybir.dt.bfloat16
ALU = mybir.AluOpType
AF = mybir.ActivationFunctionType
AX = mybir.AxisListType

DEBUG_STOP = "FULL"
_DBG = {}
MAGIC = float(np.float32(1.5 * 2 ** 23))
NEG_BIG = -1e30
D = 1024
FI = 2816
NIC = FI // 128
NT = 34
NOWN = 32
NCH = 4
CHT = 10
KW = CHT * 128
HD = 64
EPS = 1e-5


class Sem:
    def __init__(self, handle, step=1):
        self.h = handle
        self.n = 0
        self.step = step


class _Fake:
    def then_inc(self, *a, **k):
        return self


_FAKE = _Fake()


class EngProxy:
    def __init__(self, ctx, name, real):
        object.__setattr__(self, "_ctx", ctx)
        object.__setattr__(self, "_name", name)
        object.__setattr__(self, "_real", real)

    def __getattr__(self, item):
        if self._ctx.active == self._name:
            return getattr(self._real, item)
        return lambda *a, **k: _FAKE


class Ctx:
    def __init__(self):
        self.active = None
        self.ms = {}
        self.first = True

    def inc(self, instr, sem, k=None):
        k = sem.step if k is None else k
        instr.then_inc(sem.h, k)
        sem.n += k

    def wait(self, eng, sem, val):
        if self.active == eng._name and val > 0:
            eng._real.wait_ge(sem.h, val)

    def setms(self, key, val):
        if self.first:
            self.ms[key] = val
        else:
            assert self.ms[key] == val, (key, self.ms[key], val)


def build_program():
    nc = bass.Bass("TRN2", target_bir_lowering=False, detect_race_conditions=False)
    es = ExitStack()

    din = lambda n, s: nc.dram_tensor(n, s, F32, kind="ExternalInput").ap()
    x_in = din("x", [NT * 128, D])
    wq_in = din("wq", [D, D])
    wk_in = din("wk", [D, D])
    wv_in = din("wv", [D, D])
    wo_in = din("wo", [D, D])
    bqc_in = din("bqc", [128, 8])
    bkc_in = din("bkc", [128, 8])
    bvr_in = din("bvr", [1, D])
    bor_in = din("bor", [1, D])
    wg_in = din("wg", [FI, D])
    wu_in = din("wu", [FI, D])
    wd_in = din("wd", [D, FI])
    clipx_in = din("clipx", [1, 1])
    clipd_in = din("clipd", [1, 1])
    ccol_in = din("ccol", [128, 8])
    wmod_in = din("wmod", [D, 6 * D])
    eflg_in = din("eflg", [128, 2])
    ident_in = nc.dram_tensor("ident", [128, 128], BF16, kind="ExternalInput").ap()
    identf_in = nc.dram_tensor("identf", [128, 128], F32, kind="ExternalInput").ap()
    mlh_in = nc.dram_tensor("mlh", [128, 256], BF16, kind="ExternalInput").ap()
    mrh_in = nc.dram_tensor("mrh", [128, 768], BF16, kind="ExternalInput").ap()

    y_out = nc.dram_tensor("y", [NOWN * 128, D], F32, kind="ExternalOutput").ap()

    xsp = nc.dram_tensor("xsp", [NOWN * 128, D], F32).ap()
    asp = nc.dram_tensor("asp", [FI, NOWN * 128], F32).ap()
    cc1i = nc.dram_tensor("cc1i", [1, 16], F32).ap()
    cc1o = nc.dram_tensor("cc1o", [1, 16], F32).ap()
    cc2i = nc.dram_tensor("cc2i", [1, 16], F32).ap()
    cc2o = nc.dram_tensor("cc2o", [1, 16], F32).ap()

    ar_g = nc.auto_arena()
    ar_p0 = nc.auto_arena()
    ar_a = nc.auto_arena()
    ar_q = nc.auto_arena()
    ar_fgu = nc.auto_arena()
    ar_fd = nc.auto_arena()
    ar_cd = nc.auto_arena()
    ar_co = nc.auto_arena()
    ar_do = nc.auto_arena()

    g = ar_g.sbuf_tensor
    ident = g("identt", [128, 128], BF16)
    identf = g("identtf", [128, 128], F32)
    ones_bf = g("ones_bf", [1, 128], BF16)
    ones_f = g("ones_f", [1, 128], F32)
    bq8 = g("bq8", [128, 8], F32)
    bkcol = g("bkcol", [128, 8], F32)
    bkf = g("bkf", [128, 16], F32)
    eflg = g("eflg_t", [128, 2], F32)
    A2b = g("A2b", [128, D], F32)
    B2b = g("B2b", [128, D], F32)
    gbpb = g("gbpb", [128, D], F32)
    gswd2 = g("gswd2", [128, D], F32)
    swg = g("swg_t", [128, NIC], F32)
    swu = g("swu_t", [128, NIC], F32)
    swd = g("swd_t", [128, 8], F32)
    s1swg = g("s1swg", [128, NIC], F32)
    s1swu = g("s1swu", [128, NIC], F32)
    negmx16 = g("negmx16", [128, 16], F32)
    dens16 = g("dens16", [128, 16], F32)
    recs16 = g("recs16", [128, 16], F32)
    inv2c = g("inv2c", [128, NOWN], F32)
    nm2c = g("nm2c", [128, NOWN], F32)
    st12 = g("st12", [128, 12], F32)
    mv2 = g("mv2", [128, 2], F32)
    hzs = g("hzs", [128, 32], F32)
    vdum8 = hzs[:, 0:8]
    vdumB = hzs[:, 8:16]
    mvC = hzs[:, 16:18]
    lnd = hzs[:, 18:19]
    lns = hzs[:, 19:21]
    lndq = hzs[:, 21:22]
    vv2s = hzs[:, 22:23]
    epsb = hzs[:, 23:24]
    gdum = hzs[0:1, 24:25]
    vdum = hzs[:, 25:26]
    vdumS = hzs[:, 26:27]
    veps = hzs[:, 27:28]
    sdev = hzs[:, 28:29]
    inv1 = hzs[:, 29:30]
    nm1 = hzs[:, 30:31]
    ln2s = g("ln2s", [128, D], F32)
    scal1 = g("scal1", [1, 16], F32)
    s2col = g("s2col", [128, 1], F32)
    is1col = g("is1col", [128, 1], F32)
    is2col = g("is2col", [128, 1], F32)
    s1colv = g("s1colv", [128, 1], F32)
    ccst = g("ccst", [1, 16], F32)
    gmax1 = g("gmax1", [1, 1], F32)
    gmax2 = g("gmax2", [1, 1], F32)
    amax176 = g("amax176", [128, 176], F32)
    gapbf = g("gapbf", [128, D], BF16)
    gabo = g("gabo", [128, D], BF16)
    cmagP = g("cmagP", [128, 1], F32)
    cmagN = g("cmagN", [128, 1], F32)

    p0 = ar_p0.sbuf_tensor
    stg0 = [p0(f"stg0_{i}", [128, D], F32) for i in range(2)]
    wmst = p0("wmst", [128, 8 * 512], BF16)
    wmstf = p0("wmstf", [128, 8 * 512], F32)
    scf = p0("scf", [128, 8], F32)
    mod2 = p0("mod2", [1, 2 * D], F32)
    row_s = p0("row_s", [1, D], F32)
    gap1 = p0("gap1", [128, D], F32)
    gap2 = p0("gap2", [128, D], F32)
    scbf = p0("scbf", [128, 8], BF16)

    a_ = ar_a.sbuf_tensor
    wq_sb = a_("wq_sb", [128, 8 * D], BF16)
    wk_sb = a_("wk_sb", [128, 8 * D], BF16)
    wv_sb = a_("wv_sb", [128, 8 * D], BF16)
    wo2_sb = a_("wo2_sb", [128, 8 * D], BF16)
    mlh = a_("mlht", [128, 256], BF16)
    mrh = a_("mrht", [128, 768], BF16)
    bvr_bf = a_("bvr_bf", [1, D], BF16)
    A1b = a_("A1b", [128, D], BF16)
    B1b = a_("B1b", [128, D], BF16)
    accmx = a_("accmx", [128, D], F32)
    accmn = a_("accmn", [128, D], F32)
    stgA = [a_(f"stgA_{i}", [128, D], F32) for i in range(2)]
    ht1 = [a_(f"ht1_{i}", [128, 8 * 512], BF16) for i in range(2)]
    qfm = a_("qfm", [128, 8 * 1024], BF16)
    kfm = a_("kfm", [128, 8 * KW], BF16)
    vsb = a_("vsb", [128, CHT * D], BF16)
    h1t = [a_(f"h1t_{i}", [128, D], BF16) for i in range(2)]
    xA = [a_(f"xA{i}", [128, D], F32) for i in range(2)]
    probs = [a_(f"probs{i}", [128, 384], BF16) for i in range(4)]
    probsn = [a_(f"probsn{i}", [128, 384], BF16) for i in range(2)]
    probsT = [a_(f"probsT{i}", [128, 384], BF16) for i in range(2)]
    afm = a_("afm", [128, D], BF16)
    xB = stgA
    xps = xA
    cmax1 = stgA[0]
    cmax2 = stgA[1]

    q_ = ar_q.sbuf_tensor
    qstg = q_("qstg", [128, FI], F32)
    qrst = q_("qrst", [128, FI], F32)
    qrq = q_("qrq", [128, FI], BF16)
    qsw = q_("qsw", [128, 2], F32)
    qrsc = q_("qrsc", [128, 2], F32)
    qrsc2 = q_("qrsc2", [128, 2], F32)

    rwgT = ar_fgu.sbuf_tensor("rwgT", [128, 8 * FI], BF16)
    rwuT = ar_fgu.sbuf_tensor("rwuT", [128, 8 * FI], BF16)
    rwdT = ar_fd.sbuf_tensor("rwdT", [128, NIC * D], BF16)

    cd = ar_cd.sbuf_tensor
    xr = [cd(f"xr{i}", [128, D], F32) for i in range(3)]
    sgb = [cd(f"sgb{i}", [128, 512], F32) for i in range(2)]

    co = ar_co.sbuf_tensor
    rx = [co(f"rx{i}", [128, D], BF16) for i in range(2)]
    rxs = co("rxs", [128, D], F32)
    rxt = co("rxt", [128, 8 * 512], BF16)
    astg = [co(f"astg{i}", [128, 512], F32) for i in range(2)]
    A2s = co("A2s", [128, D], F32)
    B2sC = co("B2sC", [128, D], F32)

    do = ar_do.sbuf_tensor
    raG = [do(f"raG{i}", [128, NIC * 512], BF16) for i in range(2)]
    arel = [do(f"arel{i}", [128, 512], F32) for i in range(4)]
    yst = [do(f"yst{i}", [128, D], F32) for i in range(2)]

    ar_g.allocate(side="left")
    ar_p0.allocate(side="right")
    ar_p0.free()
    ar_a.allocate(side="right")
    ar_a.free()
    ar_q.allocate(side="right")
    ar_fd.allocate(side="left")
    ar_fgu.allocate(side="left")
    ar_q.free()
    ar_cd.allocate(side="right")
    ar_co.allocate(side="right")
    ar_co.free()
    ar_fgu.free()
    ar_do.allocate(side="right")

    pp = lambda *a: es.enter_context(nc.psum_tensor(*a))
    SC = [pp("SC0", [128, 512], F32), pp("SC1", [128, 512], F32)]
    TR = [pp("TR0", [128, 1024], BF16), pp("TR1", [128, 1024], BF16)]
    PV = [pp("PV0", [128, 512], F32), pp("PV1", [128, 512], F32)]
    WO = [pp("WO0", [128, 512], F32), pp("WO1", [128, 512], F32)]

    dma_sems = (["sCst", "sWo0", "sWo1", "sXA0", "sXA1", "sXB0", "sXB1", "sQl",
                 "sXr0", "sXr1", "sXr2", "sAr0", "sAr1", "sAr2", "sAr3",
                 "sXq0", "sXq1", "sCc", "sWm", "sWt", "sXs0", "sXs1",
                 "sAs0", "sAs1", "sYw0", "sYw1", "sCs", "sCb", "sCm"])
    sem_h = {n: es.enter_context(nc.semaphore(n))
             for n in (["sT", "sV", "sS", "sG", "sC", "sB"] + dma_sems)}

    C = Ctx()
    for _nm, _t in [("qfm", qfm), ("kfm", kfm), ("vsb", vsb), ("ht1_0", ht1[0]),
                    ("A1b", A1b), ("B1b", B1b), ("A2b", A2b), ("B2b", B2b),
                    ("gapbf", gapbf), ("gabo", gabo), ("gbpb", gbpb),
                    ("wo2_sb", wo2_sb), ("wq_sb", wq_sb), ("bq8", bq8),
                    ("inv2c", inv2c), ("nm2c", nm2c), ("accmx", accmx),
                    ("swg", swg), ("swd", swd), ("rwgT", rwgT), ("rwdT", rwdT),
                    ("s1swg", s1swg), ("gswd2", gswd2), ("amax176", amax176),
                    ("gmax1", gmax1), ("gmax2", gmax2), ("scal1", scal1), ("afm", afm), ("dens16", dens16), ("negmx16", negmx16), ("recs16", recs16), ("probsT0", probsT[0]), ("probsn0", probsn[0]), ("probs0", probs[0]), ("kfm", kfm)]:
        _DBG[_nm] = _t.name

    def build_all(T, V, S, Y, G, sems):
        sT, sV, sS = sems["sT"], sems["sV"], sems["sS"]
        sG, sC_, sB = sems["sG"], sems["sC"], sems["sB"]
        sCst, sWm, sWt, sCc, sCs, sQl = (sems["sCst"], sems["sWm"], sems["sWt"],
                                         sems["sCc"], sems["sCs"], sems["sQl"])
        sCb, sCm = sems["sCb"], sems["sCm"]
        sWo = [sems["sWo0"], sems["sWo1"]]
        sXA = [sems["sXA0"], sems["sXA1"]]
        sXB = [sems["sXB0"], sems["sXB1"]]
        sXr = [sems["sXr0"], sems["sXr1"], sems["sXr2"]]
        sAr = [sems["sAr0"], sems["sAr1"], sems["sAr2"], sems["sAr3"]]
        sXq = [sems["sXq0"], sems["sXq1"]]
        sXs = [sems["sXs0"], sems["sXs1"]]
        sAs = [sems["sAs0"], sems["sAs1"]]
        sYw = [sems["sYw0"], sems["sYw1"]]
        I, W, M = C.inc, C.wait, C.ms

        def Wd(eng, ms, ring=None):
            sl, v = ms
            if ring is not None:
                W(eng, ring[sl], v)
            else:
                raise RuntimeError("ring required")

        def barrier(tag):
            I(T.ldweights(ident[0:64, 0:1]), sB)
            I(V.memset(vdum[:], 0.0), sB)
            I(S.memzero(vdumS[:]), sB)
            I(G.memset(ccst[0:1, 15:16], 0.0), sB)
            C.setms(tag, sB.n)
            for e in (T, V, S, Y, G):
                W(e, sB, M[tag])
            return M[tag]

        def vgap():
            V.tensor_tensor(vdumB[:], vdum8[:], vdum8[:], ALU.add)

        def ggap():
            G.tensor_reduce(gdum[:], vdum8[:, 0:1], AX.C, ALU.add)

        def WF(eng, sem, key):
            if C.active == eng._name and not C.first:
                eng._real.wait_ge(sem.h, C.ms[key])

        I(Y.dma_start(out=ident[:], in_=ident_in), sCst)
        I(Y.dma_start(out=identf[:], in_=identf_in), sCst)
        I(Y.dma_start(out=bkcol[:], in_=bkc_in), sCst)
        I(Y.dma_start(out=eflg[:], in_=eflg_in), sCst)
        I(Y.dma_start(out=scal1[0:1, 0:1], in_=clipx_in), sCst)
        I(Y.dma_start(out=scal1[0:1, 1:2], in_=clipd_in), sCst)
        I(Y.dma_start(out=st12[:, 0:8], in_=bqc_in), sCst)
        C.setms("ldc", sCst.n)
        I(G.dma_start(out=scbf[:], in_=ccol_in), sCc)
        C.setms("ldc2", sCc.n)
        G.memset(ones_bf[:], 1.0)
        G.memset(ones_f[:], 1.0)
        G.memset(cmagP[:], MAGIC)
        G.memset(cmagN[:], -MAGIC)
        G.memset(epsb[:], EPS)
        G.memset(accmx[:], -3.0e38)
        I(G.memset(accmn[:], 3.0e38), sG)
        C.setms("msets", sG.n)

        W(S, sCc, M["ldc2"])
        S.activation(scf[:], scbf[:], AF.Silu, bias=0.0, scale=1.0)
        I(S.activation(scbf[:], scbf[:], AF.Silu, bias=0.0, scale=1.0), sS)
        C.setms("siluc", sS.n)

        W(V, sCst, M["ldc"])
        V.memset(vdum8[:], 0.0)
        V.tensor_scalar(bq8[:], st12[:, 0:8], 0.125, None, ALU.mult, ALU.bypass)
        vgap()
        V.tensor_scalar(bkf[:, 0:8], bkcol[:], eflg[:, 0:1], None, ALU.mult, ALU.bypass)
        I(V.tensor_scalar(bkf[:, 8:16], bkcol[:], eflg[:, 1:2], None, ALU.mult, ALU.bypass), sV)
        C.setms("bkf", sV.n)

        W(T, sCst, M["ldc"])
        seg_plan = [(0, B2b, False), (1, A1b, True), (2, gap1, False),
                    (3, B2b, False), (4, A2b, True), (5, gbpb, False)]
        seg_plan = [(0, B1b, False), (1, A1b, True), (2, gap1, False),
                    (3, B2b, False), (4, A2b, True), (5, gbpb, False)]
        for seg, dst, add1 in seg_plan:
            f32seg = seg in (2, 5)
            stg_t = wmstf if f32seg else wmst
            lhs_t = scf if f32seg else scbf
            for b2 in range(2):
                b = 2 * seg + b2
                if b > 0:
                    W(G, sT, M[f"modmm{b-1}"])
                for ic in range(8):
                    I(G.dma_start(out=stg_t[:, ic * 512:(ic + 1) * 512],
                                  in_=wmod_in[128 * ic:128 * (ic + 1), 512 * b:512 * (b + 1)]),
                      sWm)
                C.setms(f"ldwm{b}", sWm.n)
                W(T, sWm, M[f"ldwm{b}"])
                W(T, sS, M["siluc"])
                if b >= 2:
                    W(T, sV, M[f"modev{b-2}"])
                for ic in range(8):
                    ins = T.matmul(SC[b2][0:1, 0:512], lhs_t[:, ic:ic + 1],
                                   stg_t[:, ic * 512:(ic + 1) * 512],
                                   start=(ic == 0), stop=(ic == 7))
                I(ins, sT)
                C.setms(f"modmm{b}", sT.n)
                W(V, sT, M[f"modmm{b}"])
                I(V.tensor_copy(mod2[0:1, 512 * b2:512 * (b2 + 1)], SC[b2][0:1, 0:512]), sV)
                C.setms(f"modev{b}", sV.n)
            vgap()
            if add1:
                I(V.tensor_scalar(row_s[:], mod2[0:1, 0:D], 1.0, None, ALU.add, ALU.bypass), sV)
            else:
                I(V.tensor_copy(row_s[:], mod2[0:1, 0:D]), sV)
            C.setms(f"segrow{seg}", sV.n)
            W(T, sV, M[f"segrow{seg}"])
            if seg > 0:
                W(T, sV, M[f"segbc{seg-1}"])
            for half in range(2):
                ins = T.matmul(WO[half][:, 0:512], ones_f[:],
                               row_s[0:1, 512 * half:512 * half + 512], start=True, stop=True)
            I(ins, sT)
            C.setms(f"segmm{seg}", sT.n)
            W(V, sT, M[f"segmm{seg}"])
            for half in range(2):
                ins = V.tensor_copy(dst[:, 512 * half:512 * half + 512], WO[half][:, 0:512])
            I(ins, sV)
            C.setms(f"segbc{seg}", sV.n)

        def gatefix(src, dst, tag):
            V.tensor_scalar(gap2[:], src[:], 0.0, None, ALU.is_ge, ALU.bypass)
            I(V.tensor_scalar(gap2[:], gap2[:], 2.0, -1.0, ALU.mult, ALU.add), sV)
            C.setms(f"gf_{tag}", sV.n)
            W(S, sV, M[f"gf_{tag}"])
            I(S.activation(stg0[0][:], src[:], AF.Abs, bias=0.0, scale=1.0), sS)
            C.setms(f"ga_{tag}", sS.n)
            W(V, sS, M[f"ga_{tag}"])
            V.tensor_scalar(stg0[0][:], stg0[0][:], 0.25, None, ALU.max, ALU.bypass)
            I(V.tensor_tensor(dst[:], gap2[:], stg0[0][:], ALU.mult), sV)
            C.setms(f"gd_{tag}", sV.n)

        gatefix(gap1, gapbf, "ga")
        gatefix(gbpb, gbpb, "gb")

        W(Y, sT, M["segmm5"])
        I(Y.dma_start(out=row_s[:], in_=bor_in), sCb)
        C.setms("ldbor", sCb.n)
        W(T, sCb, M["ldbor"])
        W(T, sV, M["gd_gb"])
        for half in range(2):
            ins = T.matmul(WO[half][:, 0:512], ones_f[:],
                           row_s[0:1, 512 * half:512 * half + 512], start=True, stop=True)
        I(ins, sT)
        C.setms("bobc", sT.n)
        W(V, sT, M["bobc"])
        for half in range(2):
            V.tensor_copy(gap2[:, 512 * half:512 * half + 512], WO[half][:, 0:512])
        I(V.tensor_tensor(gabo[:], gapbf[:], gap2[:], ALU.mult), sV)
        C.setms("gabo", sV.n)

        bar0 = barrier("bar0")

        W(G, sB, bar0)
        I(G.dma_start(out=bvr_bf[:], in_=bvr_in), sWt)
        for w_sb_t, w_i in [(wq_sb, wq_in), (wk_sb, wk_in), (wv_sb, wv_in)]:
            for ic in range(8):
                I(G.dma_start(out=w_sb_t[:, ic * D:(ic + 1) * D],
                              in_=w_i[128 * ic:128 * (ic + 1), :]), sWt)
        C.setms("ldw", sWt.n)
        W(Y, sB, bar0)
        I(Y.dma_start(out=mlh[:], in_=mlh_in), sCm)
        I(Y.dma_start(out=mrh[:], in_=mrh_in), sCm)
        C.setms("ldmr", sCm.n)
        for ic in range(8):
            if ic >= 2:
                W(Y, sV, M[f"wo2_{ic-2}"])
            I(Y.dma_start(out=stgA[ic % 2][:], in_=wo_in[128 * ic:128 * (ic + 1), :]),
              sWo[ic % 2])
            C.setms(f"ldwo{ic}", sWo[ic % 2].n)
            W(V, sWo[ic % 2], M[f"ldwo{ic}"])
            I(V.tensor_tensor(wo2_sb[:, ic * D:(ic + 1) * D], stgA[ic % 2][:], gapbf[:],
                              ALU.mult), sV)
            C.setms(f"wo2_{ic}", sV.n)

        if DEBUG_STOP == "P0":
            W(G, sWt, sWt.n)
            W(Y, sCm, sCm.n)
            return
        attT = {-1: 0}
        attS = {-1: 0}
        for ch in range(NCH):
            gl0 = 8 * ch
            groups = [(0, 4), (4, 8), (8, 10)]
            vv2g = amax176[:, 0:2]
            sdevg = amax176[:, 2:4]
            invg = amax176[:, 4:6]
            nmg = amax176[:, 6:8]
            lnsg = amax176[:, 8:12]
            for gidx, (g0, g1) in enumerate(groups):
                ring = gidx % 2
                for pr0 in range(g0, g1, 2):
                  for jj in (pr0, pr0 + 1):
                    k = jj - pr0
                    lt = gl0 + jj
                    slot = jj % 2
                    if jj >= 2:
                        W(Y, sV, M[f"h1_{ch}_{jj-2}"])
                    elif ch > 0:
                        W(Y, sV, M[f"h1_{ch-1}_{jj+8}"])
                        Wd(Y, M[f"xsp_{8*ch-2}"], sXs)
                        Wd(Y, M[f"xsp_{8*ch-1}"], sXs)
                    I(Y.dma_start(out=xA[slot][:], in_=x_in[128 * lt:128 * (lt + 1), :]),
                      sXA[slot])
                    C.setms(f"xld_{ch}_{jj}", (slot, sXA[slot].n))
                    if gidx == 0 and jj == 0:
                        W(V, sT, attT[ch - 1])
                    Wd(V, M[f"xld_{ch}_{jj}"], sXA)
                    V.bn_stats(st12[:, 0:6], xA[slot][:, 0:512])
                    V.bn_stats(st12[:, 6:12], xA[slot][:, 512:1024])
                    V.bn_aggr(mv2[:], st12[:, 0:6])
                    V.bn_aggr(mvC[:], st12[:, 6:12])
                    vgap()
                    V.tensor_tensor(lnd[:], mv2[:, 0:1], mvC[:, 0:1], ALU.subtract)
                    V.tensor_tensor(lnsg[:, 2 * k:2 * k + 2], mv2[:], mvC[:], ALU.add)
                    vgap()
                    V.tensor_tensor(lndq[:], lnd[:], lnd[:], ALU.mult)
                    vgap()
                    I(V.scalar_tensor_tensor(vv2g[:, k:k + 1], lndq[:],
                                             0.5, lnsg[:, 2 * k + 1:2 * k + 2],
                                             ALU.mult, ALU.add), sV)
                    C.setms(f"ve_{ch}_{jj}", sV.n)
                  W(S, sV, M[f"ve_{ch}_{pr0+1}"])
                  if ch == 0 and pr0 == 0:
                    W(S, sG, M["msets"])
                  S.activation(sdevg[:, 0:1], vv2g[:, 0:1], AF.Sqrt, bias=epsb[:], scale=0.5)
                  I(S.activation(sdevg[:, 1:2], vv2g[:, 1:2], AF.Sqrt, bias=epsb[:],
                                 scale=0.5), sS)
                  C.setms(f"sd_{ch}_{pr0}", sS.n)
                  W(V, sS, M[f"sd_{ch}_{pr0}"])
                  V.reciprocal(invg[:, 0:1], sdevg[:, 0:1])
                  V.reciprocal(invg[:, 1:2], sdevg[:, 1:2])
                  vgap()
                  V.scalar_tensor_tensor(nmg[:, 0:1], lnsg[:, 0:1], -0.5, invg[:, 0:1],
                                         ALU.mult, ALU.mult)
                  I(V.scalar_tensor_tensor(nmg[:, 1:2], lnsg[:, 2:3], -0.5, invg[:, 1:2],
                                           ALU.mult, ALU.mult), sV)
                  C.setms(f"nm_{ch}_{pr0}", sV.n)
                  W(S, sV, M[f"nm_{ch}_{pr0}"])
                  for jj in (pr0, pr0 + 1):
                    k = jj - pr0
                    slot = jj % 2
                    I(S.activation(h1t[slot][:], xA[slot][:], AF.Identity,
                                   bias=nmg[:, k:k + 1], scale=invg[:, k:k + 1]), sS)
                    C.setms(f"lnap_{ch}_{jj}", sS.n)
                    W(V, sS, M[f"lnap_{ch}_{jj}"])
                    V.tensor_tensor(h1t[slot][:], h1t[slot][:], A1b[:], ALU.mult)
                    I(V.tensor_tensor(h1t[slot][:], h1t[slot][:], B1b[:], ALU.add), sV)
                    C.setms(f"h1_{ch}_{jj}", sV.n)
                    W(T, sV, M[f"h1_{ch}_{jj}"])
                    if jj >= 2:
                        W(T, sV, M[f"htev_{ch}_{jj-2}"])
                    for fc in range(8):
                        ins = T.transpose(TR[jj % 2][:, fc * 128:(fc + 1) * 128],
                                          h1t[slot][:, fc * 128:(fc + 1) * 128], ident[:])
                    I(ins, sT)
                    C.setms(f"httr_{ch}_{jj}", sT.n)
                    W(V, sT, M[f"httr_{ch}_{jj}"])
                    if gidx == 2 and jj == g0:
                        W(V, sT, M[f"vmm_{ch}_{3}"])
                    for fc in range(8):
                        ins = V.tensor_copy(
                            ht1[ring][:, fc * 512 + (jj - g0) * 128:
                                       fc * 512 + (jj - g0) * 128 + 128],
                            TR[jj % 2][:, fc * 128:(fc + 1) * 128])
                    I(ins, sV)
                    C.setms(f"htev_{ch}_{jj}", sV.n)
                    if DEBUG_STOP == "H10" and ch == 0 and jj == 0:
                        W(G, sV, M[f"htev_{ch}_{jj}"])
                        for dout, dsrc in [
                            (y_out[0:128, :], xA[0][:]),
                            (y_out[128:256, :], h1t[0][:]),
                            (y_out[256:384, 0:2], mv2[:]),
                            (y_out[256:384, 2:3], inv1[:]),
                            (y_out[256:384, 3:4], nm1[:]),
                            (y_out[256:384, 4:16], st12[:]),
                            (y_out[384:512, 0:1024], ht1[0][:, 0:1024]),
                            (y_out[512:640, :], A1b[:]),
                            (y_out[640:768, :], B1b[:]),
                            (y_out[768:896, :], A2b[:]),
                            (y_out[896:1024, :], gapbf[:]),
                            (y_out[1024:1152, :], gbpb[:]),
                            (y_out[1152:1280, :], gabo[:]),
                            (y_out[1280:1408, 0:8], scbf[:]),
                        ]:
                            I(G.dma_start(out=dout, in_=dsrc,
                                          allow_slow_non_contiguous=True), sCs)
                        W(G, sCs, sems["sCs"].n)
                        return

                gN = (g1 - g0) * 128
                ht_ready = M[f"htev_{ch}_{g1-1}"]
                rr = ring

                def h1sl(ic, col0, width, _r=rr):
                    return ht1[_r][:, ic * 512 + col0: ic * 512 + col0 + width]

                own_lo = max(g0, 1)
                own_hi = min(g1, 9)
                if own_hi > own_lo:
                    qcol0 = (own_lo - g0) * 128
                    qN = (own_hi - own_lo) * 128
                    for oc in range(8):
                        W(T, sV, ht_ready)
                        if oc >= 2:
                            W(T, sV, M[f"qev_{ch}_{g0}_{oc-2}"])
                        elif ch == 0 and gidx == 0 and oc == 0:
                            W(T, sWt, M["ldw"])
                        elif ch > 0 and gidx == 0:
                            W(T, sS, attS[ch - 1])
                        for ic in range(8):
                            ins = T.matmul(SC[oc % 2][:, 0:qN],
                                           wq_sb[:, ic * D + oc * 128: ic * D + oc * 128 + 128],
                                           h1sl(ic, qcol0, qN), start=(ic == 0), stop=(ic == 7))
                        I(ins, sT)
                        C.setms(f"qmm_{ch}_{g0}_{oc}", sT.n)
                        W(V, sT, M[f"qmm_{ch}_{g0}_{oc}"])
                        dst0 = (own_lo - 1) * 128
                        I(V.tensor_scalar(qfm[:, oc * 1024 + dst0: oc * 1024 + dst0 + qN],
                                          SC[oc % 2][:, 0:qN], 0.125, bq8[:, oc:oc + 1],
                                          ALU.mult, ALU.add), sV)
                        C.setms(f"qev_{ch}_{g0}_{oc}", sV.n)
                for oc in range(8):
                    W(T, sV, ht_ready)
                    if oc >= 2:
                        W(T, sV, M[f"kev_{ch}_{g0}_{oc-2}"])
                    elif own_hi > own_lo:
                        W(T, sV, M[f"qev_{ch}_{g0}_{6+oc}"])
                    elif ch == 0 and gidx == 0 and oc == 0:
                        W(T, sWt, M["ldw"])
                    elif ch > 0 and gidx == 0 and oc == 0:
                        W(T, sS, attS[ch - 1])
                    for ic in range(8):
                        ins = T.matmul(SC[oc % 2][:, 0:gN],
                                       wk_sb[:, ic * D + oc * 128: ic * D + oc * 128 + 128],
                                       h1sl(ic, 0, gN), start=(ic == 0), stop=(ic == 7))
                    I(ins, sT)
                    C.setms(f"kmm_{ch}_{g0}_{oc}", sT.n)
                    W(V, sT, M[f"kmm_{ch}_{g0}_{oc}"])
                    kc0 = oc * KW + g0 * 128
                    if ch == 0 and g0 == 0:
                        I(V.tensor_scalar(kfm[:, kc0:kc0 + 128], SC[oc % 2][:, 0:128],
                                          eflg[:, 0:1], bkf[:, oc:oc + 1], ALU.mult, ALU.add), sV)
                        I(V.tensor_scalar(kfm[:, kc0 + 128:kc0 + gN], SC[oc % 2][:, 128:gN],
                                          1.0, bkcol[:, oc:oc + 1], ALU.mult, ALU.add), sV)
                    elif ch == NCH - 1 and g0 == 8:
                        I(V.tensor_scalar(kfm[:, kc0:kc0 + gN - 128], SC[oc % 2][:, 0:gN - 128],
                                          1.0, bkcol[:, oc:oc + 1], ALU.mult, ALU.add), sV)
                        I(V.tensor_scalar(kfm[:, kc0 + gN - 128:kc0 + gN],
                                          SC[oc % 2][:, gN - 128:gN],
                                          eflg[:, 1:2], bkf[:, 8 + oc:9 + oc], ALU.mult, ALU.add),
                          sV)
                    else:
                        I(V.tensor_scalar(kfm[:, kc0:kc0 + gN], SC[oc % 2][:, 0:gN],
                                          1.0, bkcol[:, oc:oc + 1], ALU.mult, ALU.add), sV)
                    C.setms(f"kev_{ch}_{g0}_{oc}", sV.n)
                for jj in range(g0, g1):
                    W(T, sV, ht_ready)
                    if jj >= 1:
                        W(T, sV, M[f"vev_{ch}_{jj-1}"])
                    for ic in range(8):
                        for half in range(2):
                            T.matmul(PV[half][:, 0:512], h1sl(ic, (jj - g0) * 128, 128),
                                     wv_sb[:, ic * D + 512 * half: ic * D + 512 * half + 512],
                                     start=(ic == 0), stop=False)
                    for half in range(2):
                        ins = T.matmul(PV[half][:, 0:512], ones_bf[:],
                                       bvr_bf[0:1, 512 * half:512 * half + 512],
                                       start=False, stop=True)
                    I(ins, sT)
                    C.setms(f"vmm_{ch}_{jj}", sT.n)
                    W(V, sT, M[f"vmm_{ch}_{jj}"])
                    edge = (ch == 0 and jj == 0) or (ch == NCH - 1 and jj == CHT - 1)
                    for half in range(2):
                        if edge:
                            ecol = 0 if ch == 0 else 1
                            ins = V.tensor_scalar(
                                vsb[:, jj * D + 512 * half: jj * D + 512 * half + 512],
                                PV[half][:, 0:512], eflg[:, ecol:ecol + 1], None,
                                ALU.mult, ALU.bypass)
                        else:
                            ins = V.tensor_copy(
                                vsb[:, jj * D + 512 * half: jj * D + 512 * half + 512],
                                PV[half][:, 0:512])
                    I(ins, sV)
                    C.setms(f"vev_{ch}_{jj}", sV.n)
                if DEBUG_STOP == "QKV0" and ch == 0 and gidx == 0:
                    return
            qkvdone = sV.n
            C.setms(f"qkvdone_{ch}", qkvdone)
            if DEBUG_STOP == "QKVC0" and ch == 0:
                return

            for jj in range(1, 9):
                lt = gl0 + jj
                oti = 8 * ch + jj - 1
                qb = oti
                qc0 = (jj - 1) * 128
                wc0 = (jj - 1) * 128
                slotB = oti % 2
                if oti >= 2:
                    W(Y, sV, M[f"xBfree_{oti-2}"])
                elif oti == 0:
                    W(Y, sV, M["wo2_7"])
                I(Y.dma_start(out=xB[slotB][:], in_=x_in[128 * lt:128 * (lt + 1), :]),
                  sXB[slotB])
                C.setms(f"xBld_{oti}", (slotB, sXB[slotB].n))

                SCX = [SC[0], SC[1], WO[0], WO[1]]
                for bbb in range(5):
                  if bbb < 4:
                    for hh in range(4 * bbb, 4 * bbb + 4):
                        hp = hh % 2
                        hb = hh % 4
                        b = 64 * hp
                        fc = hh // 2
                        if hh >= 4:
                            W(T, sS, M[f"exp_{qb}_{hh-4}"])
                        elif qb > 0:
                            if hh < 2:
                                W(T, sS, M[f"exp_{qb-1}_{12+hh}"])
                            else:
                                W(T, sV, M[f"xps_{qb-1}"])
                        if jj == 1 and hh < 2:
                            W(T, sV, qkvdone)
                            if ch == 0 and hh == 0:
                                W(T, sCm, M["ldmr"])
                        T.matmul(SCX[hb][:, 0:384],
                                 qfm[b:b + 64, fc * 1024 + qc0: fc * 1024 + qc0 + 128],
                                 kfm[b:b + 64, fc * KW + wc0: fc * KW + wc0 + 384],
                                 start=True, stop=False)
                        T.matmul(SCX[hb][:, 0:384], mlh[b:b + 64, 0:128],
                                 mrh[b:b + 64, 0:384], start=False, stop=False)
                        ins = T.matmul(SCX[hb][:, 0:384], mlh[b:b + 64, 128:256],
                                       mrh[b:b + 64, 384:768], start=False, stop=True)
                        I(ins, sT)
                        C.setms(f"sc_{qb}_{hh}", sT.n)
                        W(V, sT, M[f"sc_{qb}_{hh}"])
                        I(V.reduce_max(negmx16[:, hh:hh + 1], SCX[hb][:, 0:384], AX.X,
                                       negate=True), sV)
                        C.setms(f"mx_{qb}_{hh}", sV.n)
                        W(S, sV, M[f"mx_{qb}_{hh}"])
                        if hh >= 4:
                            WF(S, sT, f"ptr_{qb}_{hh-4}")
                        elif qb > 0:
                            W(S, sT, M[f"ptr_{qb-1}_{12+hh}"])
                        I(S.activation(probs[hh % 4][:], SCX[hb][:, 0:384], AF.Exp,
                                       bias=negmx16[:, hh:hh + 1], scale=1.0,
                                       accum_out=dens16[:, hh:hh + 1]), sS)
                        C.setms(f"exp_{qb}_{hh}", sS.n)
                  if bbb >= 1:
                    for h2 in range(4 * (bbb - 1), 4 * (bbb - 1) + 4):
                        hp2 = h2 % 2
                        W(T, sV, M[f"nrm_{qb}_{h2}"])
                        if h2 >= 2:
                            W(T, sV, M[f"ptev_{qb}_{h2-2}"])
                        for j3 in range(3):
                            ins = T.transpose(TR[hp2][:, j3 * 128:(j3 + 1) * 128],
                                              probs[h2 % 4][:, j3 * 128:(j3 + 1) * 128],
                                              ident[:])
                        I(ins, sT)
                        C.setms(f"ptr_{qb}_{h2}", sT.n)
                        W(V, sT, M[f"ptr_{qb}_{h2}"])
                        if h2 >= 2:
                            W(V, sT, M[f"pv_{qb}_{h2-2}"])
                        I(V.tensor_copy(probsT[h2 % 2][:], TR[hp2][:, 0:384]), sV)
                        C.setms(f"ptev_{qb}_{h2}", sV.n)
                        W(T, sV, M[f"ptev_{qb}_{h2}"])
                        if h2 in (0, 1, 8, 9) and qb > 8 * ch:
                            W(T, sV, M[f"afmev_{qb-1}"])
                        pvb = PV[h2 // 8]
                        foff = 128 * ((h2 // 2) % 4)
                        hp3 = h2 % 2
                        for j3 in range(3):
                            wt = jj - 1 + j3
                            ins = T.matmul(
                                pvb[64 * hp3:64 * hp3 + 64, foff:foff + 128],
                                vsb[:, wt * D + HD * h2: wt * D + HD * h2 + HD],
                                probsT[h2 % 2][:, j3 * 128:(j3 + 1) * 128],
                                start=(j3 == 0), stop=(j3 == 2))
                        I(ins, sT)
                        C.setms(f"pv_{qb}_{h2}", sT.n)
                  if bbb < 4:
                    hh3 = 4 * bbb + 3
                    W(V, sS, M[f"exp_{qb}_{hh3}"])
                    V.reciprocal(recs16[:, 4 * bbb:4 * bbb + 4],
                                 dens16[:, 4 * bbb:4 * bbb + 4])
                    vgap()
                    vgap()
                    for h2 in range(4 * bbb, 4 * bbb + 4):
                        if h2 >= 4:
                            W(V, sT, M[f"ptr_{qb}_{h2-4}"])
                        elif qb > 0:
                            W(V, sT, M[f"ptr_{qb-1}_{12+h2}"])
                        I(V.tensor_scalar(probs[h2 % 4][:], probs[h2 % 4][:],
                                          recs16[:, h2:h2 + 1], None,
                                          ALU.mult, ALU.bypass), sV)
                        C.setms(f"nrm_{qb}_{h2}", sV.n)
                W(V, sT, M[f"pv_{qb}_{15}"])
                V.tensor_copy(afm[:, 0:512], PV[0][:, 0:512])
                I(V.tensor_copy(afm[:, 512:1024], PV[1][:, 0:512]), sV)
                C.setms(f"afmev_{qb}", sV.n)
                W(T, sV, M[f"afmev_{qb}"])
                if qb > 0:
                    W(T, sV, M[f"xps_{qb-1}"])
                for p in range(8):
                    for half in range(2):
                        T.matmul(WO[half][:, 0:512], afm[:, p * 128:(p + 1) * 128],
                                 wo2_sb[:, p * D + 512 * half: p * D + 512 * half + 512],
                                 start=(p == 0), stop=False)
                for half in range(2):
                    ins = T.matmul(WO[half][:, 0:512], ones_bf[:],
                                   gabo[0:1, 512 * half:512 * half + 512], start=False, stop=True)
                I(ins, sT)
                C.setms(f"womm_{qb}", sT.n)
                slotP = oti % 2
                W(V, sT, M[f"womm_{qb}"])
                Wd(V, M[f"xBld_{oti}"], sXB)
                if oti >= 2:
                    Wd(V, M[f"xsp_{oti-2}"], sXs)
                for half in range(2):
                    ins = V.tensor_tensor(xps[slotP][:, 512 * half:512 * half + 512],
                                          WO[half][:, 0:512],
                                          xB[slotB][:, 512 * half:512 * half + 512], ALU.add)
                I(ins, sV)
                C.setms(f"xps_{qb}", sV.n)
                C.setms(f"xBfree_{oti}", sV.n)
                V.bn_stats(st12[:, 0:6], xps[slotP][:, 0:512])
                V.bn_stats(st12[:, 6:12], xps[slotP][:, 512:1024])
                V.bn_aggr(mv2[:], st12[:, 0:6])
                V.bn_aggr(mvC[:], st12[:, 6:12])
                vgap()
                V.tensor_tensor(lnd[:], mv2[:, 0:1], mvC[:, 0:1], ALU.subtract)
                V.tensor_tensor(lns[:], mv2[:], mvC[:], ALU.add)
                vgap()
                V.tensor_tensor(lndq[:], lnd[:], lnd[:], ALU.mult)
                vgap()
                I(V.scalar_tensor_tensor(vv2s[:], lndq[:], 0.5, lns[:, 1:2],
                                         ALU.mult, ALU.add), sV)
                C.setms(f"ve2_{oti}", sV.n)
                W(S, sV, M[f"ve2_{oti}"])
                I(S.activation(sdev[:], vv2s[:], AF.Sqrt, bias=epsb[:], scale=0.5), sS)
                C.setms(f"sd2_{oti}", sS.n)
                W(V, sS, M[f"sd2_{oti}"])
                V.reciprocal(inv2c[:, oti:oti + 1], sdev[:])
                vgap()
                I(V.scalar_tensor_tensor(nm2c[:, oti:oti + 1], lns[:, 0:1], -0.5,
                                         inv2c[:, oti:oti + 1], ALU.mult, ALU.mult), sV)
                C.setms(f"nm2_{oti}", sV.n)
                W(S, sV, M[f"nm2_{oti}"])
                I(S.activation(ln2s[:], xps[slotP][:], AF.Identity,
                               bias=nm2c[:, oti:oti + 1], scale=inv2c[:, oti:oti + 1]), sS)
                C.setms(f"ln2ap_{oti}", sS.n)
                W(V, sS, M[f"ln2ap_{oti}"])
                V.tensor_tensor(accmx[:], accmx[:], ln2s[:], ALU.max)
                I(V.tensor_tensor(accmn[:], accmn[:], ln2s[:], ALU.min), sV)
                C.setms(f"ln2_{oti}", sV.n)
                W(G, sV, M[f"xps_{qb}"])
                I(G.dma_start(out=xsp[128 * oti:128 * (oti + 1), :], in_=xps[slotP][:]),
                  sXs[slotP])
                C.setms(f"xsp_{oti}", (slotP, sXs[slotP].n))
                if DEBUG_STOP == f"ATT{qb}":
                    Wd(G, M[f"xsp_{oti}"], sXs)
                    W(G, sV, sems["sV"].n)
                    W(G, sS, sems["sS"].n)
                    for dout, dsrc in [
                        (y_out[0:128, :], xps[slotP][:]),
                        (y_out[128:256, :], afm[:]),
                        (y_out[256:384, 0:16], negmx16[:]),
                        (y_out[256:384, 16:32], dens16[:]),
                        (y_out[256:384, 32:48], recs16[:]),
                        (y_out[384:512, 0:384], probs[0][:]),
                        (y_out[512:640, 0:384], probs[1][:]),
                        (y_out[640:768, 0:384], probs[2][:]),
                        (y_out[768:896, 0:384], probs[3][:]),
                        (y_out[896:1024, 0:384], probsT[0][:]),
                        (y_out[1024:1152, 0:384], probsT[1][:]),
                        (y_out[1152:1280, 0:1024], qfm[:, 0:1024]),
                        (y_out[1280:1408, 0:1024], kfm[:, 0:1024]),
                        (y_out[1408:1536, 0:1024], vsb[:, 1024:2048]),
                    ]:
                        I(G.dma_start(out=dout, in_=dsrc,
                                      allow_slow_non_contiguous=True), sCs)
                    W(G, sCs, sems["sCs"].n)
                    return
            attT[ch] = sT.n
            attS[ch] = sS.n
        C.setms("xs_tot0", sXs[0].n)
        C.setms("xs_tot1", sXs[1].n)
        if False:
            pass

        if DEBUG_STOP == "A":
            W(G, sems["sXs0"], sXs[0].n)
            W(G, sems["sXs1"], sXs[1].n)
            I(G.dma_start(out=y_out, in_=xsp), sCs)
            C.setms("dbgA", sCs.n)
            W(G, sCs, M["dbgA"])
            return
        V.tensor_tensor(cmax1[:], accmx[:], A2b[:], ALU.mult)
        I(V.tensor_tensor(cmax1[:], cmax1[:], B2b[:], ALU.add), sV)
        C.setms("cm1a", sV.n)
        W(S, sV, M["cm1a"])
        I(S.activation(cmax1[:], cmax1[:], AF.Abs, bias=0.0, scale=1.0), sS)
        C.setms("cm1", sS.n)
        V.tensor_tensor(cmax2[:], accmn[:], A2b[:], ALU.mult)
        I(V.tensor_tensor(cmax2[:], cmax2[:], B2b[:], ALU.add), sV)
        C.setms("cm2a", sV.n)
        W(S, sV, M["cm2a"])
        I(S.activation(cmax2[:], cmax2[:], AF.Abs, bias=0.0, scale=1.0), sS)
        C.setms("cm2", sS.n)
        W(V, sS, M["cm2"])
        V.tensor_tensor(cmax1[:], cmax1[:], cmax2[:], ALU.max)
        I(V.reduce_max(veps[:], cmax1[:], AX.X), sV)
        C.setms("lmcol", sV.n)
        W(G, sV, M["lmcol"])
        G.memset(ccst[0:1, 1:15], 0.0)
        I(G.tensor_reduce(ccst[0:1, 0:1], veps[:], AX.C, ALU.max), sG)
        C.setms("lmg", sG.n)
        ggap()
        ggap()
        I(G.dma_start(out=cc1i, in_=ccst[:]), sCs)
        C.setms("cc1w", sCs.n)
        W(G, sCs, M["cc1w"])
        I(G.collective_compute("AllReduce", ALU.max, replica_groups=[list(range(8))],
                               ins=[cc1i.opt()], outs=[cc1o.opt()]), sC_)
        C.setms("cc1", sC_.n)

        bar1 = barrier("bar1")

        qprev = [(None, None, None)]
        sQlr = [sQl, sXA[0]]

        def quant(w_i, nch, width, dstT, sw_t, qx):
            ntr = width // 128
            dbl = qx < 2
            for ci in range(nch):
                p = ci % 2
                qoff = p * width if dbl else 0
                back = 2 if dbl else 1
                gci = ci - back
                if gci >= 0:
                    W(Y, sS, M[f"qrnd_{qx}_{gci}"])
                elif qx == 1:
                    W(Y, sS, M[f"qrnd_0_{NIC+gci}"])
                elif qx == 2:
                    W(Y, sS, M[f"qrnd_1_{NIC-1}"])
                else:
                    W(Y, sB, bar1)
                I(Y.dma_start(out=qstg[:, qoff:qoff + width],
                              in_=w_i[128 * ci:128 * (ci + 1), :]), sQlr[p])
                C.setms(f"qld_{qx}_{ci}", sQlr[p].n)
                W(V, sQlr[p], M[f"qld_{qx}_{ci}"])
                I(V.tensor_reduce(qsw[:, p:p + 1], qstg[:, qoff:qoff + width], AX.X,
                                  ALU.max, apply_absolute_value=True), sV)
                vgap()
                V.reciprocal(qrsc[:, p:p + 1], qsw[:, p:p + 1])
                V.tensor_scalar(sw_t[:, ci:ci + 1], qsw[:, p:p + 1], 1.0 / 127.0, None,
                                ALU.mult, ALU.bypass)
                I(V.tensor_scalar(qrsc2[:, p:p + 1], qrsc[:, p:p + 1], 127.0, None,
                                  ALU.mult, ALU.bypass), sV)
                C.setms(f"qrs_{qx}_{ci}", sV.n)
                W(S, sV, M[f"qrs_{qx}_{ci}"])
                if qx == 0 and ci == 0:
                    W(S, sG, M["msets"])
                else:
                    W(S, sG, M["qclip_prev"])
                S.activation(qrst[:, 0:width], qstg[:, qoff:qoff + width], AF.Identity,
                             bias=cmagP[:], scale=qrsc2[:, p:p + 1])
                I(S.activation(qrst[:, 0:width], qrst[:, 0:width], AF.Identity,
                               bias=cmagN[:], scale=1.0), sS)
                C.setms(f"qrnd_{qx}_{ci}", sS.n)
                W(G, sS, M[f"qrnd_{qx}_{ci}"])
                if not (qx == 0 and ci == 0):
                    W(G, sT, M["qtr_prev_last"])
                I(G.tensor_scalar(qrq[:, 0:width], qrst[:, 0:width], 127.0, -128.0,
                                  ALU.min, ALU.max), sG)
                C.setms(f"qclip_{qx}_{ci}", sG.n)
                C.ms["qclip_prev"] = M[f"qclip_{qx}_{ci}"]
                W(T, sG, M[f"qclip_{qx}_{ci}"])
                if ci == 0 and qx == 0:
                    W(T, sB, bar1)
                for tb in range(ntr):
                    src_ap = TR[tb % 2][:, 0:128]
                    if tb >= 2:
                        W(T, sV if (tb - 2) % 2 == 0 else sS, M[f"qtev_{qx}_{ci}_{tb-2}"])
                    else:
                        pq, pc, pn = qprev[0]
                        if pq is not None:
                            lu = pn - 2 + tb
                            W(T, sV if lu % 2 == 0 else sS, M[f"qtev_{pq}_{pc}_{lu}"])
                    ins = T.transpose(src_ap, qrq[:, tb * 128:(tb + 1) * 128],
                                      ident[:])
                    I(ins, sT)
                    C.setms(f"qtr_{qx}_{ci}_{tb}", sT.n)
                    if qx < 2:
                        dst = dstT[:, tb * FI + ci * 128: tb * FI + (ci + 1) * 128]
                    else:
                        dst = dstT[:, tb * D + ci * 128: tb * D + (ci + 1) * 128]
                    if tb % 2 == 0:
                        W(V, sT, M[f"qtr_{qx}_{ci}_{tb}"])
                        I(V.tensor_copy(dst, src_ap), sV)
                        C.setms(f"qtev_{qx}_{ci}_{tb}", sV.n)
                    else:
                        W(S, sT, M[f"qtr_{qx}_{ci}_{tb}"])
                        I(S.copy(dst, src_ap), sS)
                        C.setms(f"qtev_{qx}_{ci}_{tb}", sS.n)
                qprev[0] = (qx, ci, ntr)
                C.ms["qtr_prev_last"] = M[f"qtr_{qx}_{ci}_{ntr-1}"]

        quant(wg_in, NIC, D, rwgT, swg, 0)
        quant(wu_in, NIC, D, rwuT, swu, 1)
        quant(wd_in, 8, FI, rwdT, swd, 2)

        W(G, sC_, M["cc1"])
        I(G.dma_start(out=gmax1[:], in_=cc1o[0:1, 0:1]), sCc)
        C.setms("gm1", sCc.n)
        I(S.activation(scal1[0:1, 2:3], scal1[0:1, 0:1], AF.Sigmoid, bias=0.0, scale=1.0), sS)
        C.setms("sig1", sS.n)
        W(V, sS, M["sig1"])
        W(V, sCc, M["gm1"])
        V.tensor_tensor(scal1[0:1, 4:5], scal1[0:1, 2:3], gmax1[:], ALU.mult)
        vgap()
        vgap()
        V.reciprocal(scal1[0:1, 6:7], scal1[0:1, 4:5])
        V.tensor_scalar(scal1[0:1, 5:6], scal1[0:1, 4:5], 1.0 / 127.0, None, ALU.mult, ALU.bypass)
        vgap()
        I(V.tensor_scalar(scal1[0:1, 7:8], scal1[0:1, 6:7], 127.0, None, ALU.mult, ALU.bypass),
          sV)
        C.setms("s1done", sV.n)
        W(T, sV, M["s1done"])
        T.matmul(WO[0][:, 0:1], ones_f[:], scal1[0:1, 5:6], start=True, stop=True)
        I(T.matmul(WO[1][:, 0:1], ones_f[:], scal1[0:1, 7:8], start=True, stop=True), sT)
        C.setms("s1bc", sT.n)
        W(V, sT, M["s1bc"])
        V.tensor_copy(s1colv[:], WO[0][:, 0:1])
        I(V.tensor_copy(is1col[:], WO[1][:, 0:1]), sV)
        C.setms("s1col", sV.n)
        vgap()
        vgap()
        V.tensor_scalar(s1swg[:], swg[:], s1colv[:], None, ALU.mult, ALU.bypass)
        V.tensor_scalar(s1swu[:], swu[:], s1colv[:], None, ALU.mult, ALU.bypass)
        V.tensor_scalar(A2s[:], A2b[:], is1col[:], None, ALU.mult, ALU.bypass)
        I(V.tensor_scalar(B2sC[:], B2b[:], is1col[:], MAGIC, ALU.mult, ALU.add), sV)
        C.setms("cprep", sV.n)

        bar2 = barrier("bar2")
        if DEBUG_STOP == "QUANT":
            W(G, sems["sXs0"], sXs[0].n)
            W(G, sems["sXs1"], sXs[1].n)
            I(G.dma_start(out=y_out, in_=xsp), sCs)
            C.setms("dbgQ", sCs.n)
            W(G, sCs, M["dbgQ"])
            return

        for gi in range(8):
            for st in range(4):
                tt = 4 * gi + st
                sl = tt % 3
                if tt >= 3:
                    W(Y, sV, M[f"rx_{tt-3}"])
                elif tt == 0:
                    W(Y, sB, bar2)
                    W(Y, sems["sXs0"], C.ms["xs_tot0"])
                    W(Y, sems["sXs1"], C.ms["xs_tot1"])
                I(Y.dma_start(out=xr[sl][:], in_=xsp[128 * tt:128 * (tt + 1), :]), sXr[sl])
                C.setms(f"xrld_{tt}", (sl, sXr[sl].n))
                Wd(V, M[f"xrld_{tt}"], sXr)
                if tt == 0:
                    W(V, sB, bar2)
                V.tensor_scalar(ln2s[:], xr[sl][:], inv2c[:, tt:tt + 1], nm2c[:, tt:tt + 1],
                                ALU.mult, ALU.add)
                V.tensor_tensor(ln2s[:], ln2s[:], A2s[:], ALU.mult)
                V.tensor_tensor(ln2s[:], ln2s[:], B2sC[:], ALU.add)
                V.tensor_scalar(rxs[:], ln2s[:], MAGIC, 127.0, ALU.subtract, ALU.min)
                I(V.tensor_scalar(rx[st % 2][:], rxs[:], -128.0, None, ALU.max, ALU.bypass), sV)
                C.setms(f"rx_{tt}", sV.n)
                W(T, sV, M[f"rx_{tt}"])
                if tt >= 2:
                    W(T, sV, M[f"rxtev_{tt-2}"])
                for fc in range(8):
                    ins = T.transpose(TR[tt % 2][:, fc * 128:(fc + 1) * 128],
                                      rx[st % 2][:, fc * 128:(fc + 1) * 128], ident[:])
                I(ins, sT)
                C.setms(f"rxtr_{tt}", sT.n)
                W(V, sT, M[f"rxtr_{tt}"])
                if st == 0 and gi > 0:
                    W(V, sT, M[f"gumm_{gi-1}_{NIC-1}"])
                for fc in range(8):
                    ins = V.tensor_copy(rxt[:, fc * 512 + st * 128: fc * 512 + st * 128 + 128],
                                        TR[tt % 2][:, fc * 128:(fc + 1) * 128])
                I(ins, sV)
                C.setms(f"rxtev_{tt}", sV.n)
            rxtrdy = M[f"rxtev_{4*gi+3}"]
            for ic in range(NIC):
                bsel = ic % 2
                W(T, sV, rxtrdy)
                if ic >= 2:
                    W(T, sS, M[f"silu_{gi}_{ic-2}"])
                    W(T, sV, M[f"amade_{gi}_{ic-2}"])
                elif gi > 0 and ic == 0:
                    W(T, sS, M[f"silu_{gi-1}_{NIC-1}"])
                    W(T, sV, M[f"amade_{gi-1}_{NIC-1}"])
                for fc in range(8):
                    T.matmul(SC[bsel][:, 0:512],
                             rwgT[:, fc * FI + ic * 128: fc * FI + (ic + 1) * 128],
                             rxt[:, fc * 512:(fc + 1) * 512], start=(fc == 0), stop=(fc == 7))
                for fc in range(8):
                    ins = T.matmul(PV[bsel][:, 0:512],
                                   rwuT[:, fc * FI + ic * 128: fc * FI + (ic + 1) * 128],
                                   rxt[:, fc * 512:(fc + 1) * 512], start=(fc == 0), stop=(fc == 7))
                I(ins, sT)
                C.setms(f"gumm_{gi}_{ic}", sT.n)
                W(S, sT, M[f"gumm_{gi}_{ic}"])
                I(S.activation(sgb[bsel][:], SC[bsel][:, 0:512], AF.Silu,
                               bias=0.0, scale=s1swg[:, ic:ic + 1]), sS)
                C.setms(f"silu_{gi}_{ic}", sS.n)
                asl = ic % 2
                W(V, sS, M[f"silu_{gi}_{ic}"])
                if ic >= 2:
                    Wd(V, M[f"aspw_{gi}_{ic-2}"], sAs)
                elif gi > 0 and ic < 2:
                    Wd(V, M[f"aspw_{gi-1}_{NIC-2+ic}"], sAs)
                I(V.scalar_tensor_tensor(astg[asl][:], PV[bsel][:, 0:512],
                                         s1swu[:, ic:ic + 1], sgb[bsel][:],
                                         ALU.mult, ALU.mult), sV)
                C.setms(f"amade_{gi}_{ic}", sV.n)
                I(V.tensor_reduce(amax176[:, gi * NIC + ic: gi * NIC + ic + 1],
                                  astg[asl][:], AX.X, ALU.max, apply_absolute_value=True), sV)
                C.setms(f"ardc_{gi}_{ic}", sV.n)
                W(G, sV, M[f"ardc_{gi}_{ic}"])
                I(G.dma_start(out=asp[128 * ic:128 * (ic + 1), 512 * gi:512 * (gi + 1)],
                              in_=astg[asl][:]), sAs[asl])
                C.setms(f"aspw_{gi}_{ic}", (asl, sAs[asl].n))

        if DEBUG_STOP == "C":
            W(G, sems["sAs0"], sAs[0].n)
            W(G, sems["sAs1"], sAs[1].n)
            I(G.dma_start(out=y_out.rearrange("(a b) d -> a (b d)", a=1024),
                          in_=asp[0:1024, :]), sCs)
            C.setms("dbgC", sCs.n)
            W(G, sCs, M["dbgC"])
            return
        C.setms("as_tot0", sAs[0].n)
        C.setms("as_tot1", sAs[1].n)
        vgap()
        vgap()
        I(V.tensor_reduce(veps[:], amax176[:], AX.X, ALU.max), sV)
        C.setms("amcol", sV.n)
        W(G, sV, M["amcol"])
        G.memset(ccst[0:1, 1:15], 0.0)
        I(G.tensor_reduce(ccst[0:1, 0:1], veps[:], AX.C, ALU.max), sG)
        ggap()
        ggap()
        I(G.dma_start(out=cc2i, in_=ccst[:]), sCs)
        C.setms("cc2w", sCs.n)
        W(G, sCs, M["cc2w"])
        I(G.collective_compute("AllReduce", ALU.max, replica_groups=[list(range(8))],
                               ins=[cc2i.opt()], outs=[cc2o.opt()]), sC_)
        C.setms("cc2", sC_.n)
        W(G, sC_, M["cc2"])
        I(G.dma_start(out=gmax2[:], in_=cc2o[0:1, 0:1]), sCc)
        C.setms("gm2", sCc.n)
        W(T, sS, M[f"silu_{7}_{NIC-1}"])
        for r in range(8):
            bk = SC[r // 4]
            ins = T.transpose(bk[0:1, 128 * (r % 4):128 * (r % 4 + 1)],
                              swd[:, r:r + 1], identf[:])
        I(ins, sT)
        C.setms("swdT", sT.n)
        W(V, sT, M["swdT"])
        V.tensor_copy(ln2s[0:1, 0:512], SC[0][0:1, 0:512])
        I(V.tensor_copy(ln2s[0:1, 512:1024], SC[1][0:1, 0:512]), sV)
        C.setms("swdTe", sV.n)
        W(T, sV, M["swdTe"])
        for half in range(2):
            ins = T.matmul(WO[half][:, 0:512], ones_f[:],
                           ln2s[0:1, 512 * half:512 * half + 512], start=True, stop=True)
        I(ins, sT)
        C.setms("swb", sT.n)
        W(V, sT, M["swb"])
        for half in range(2):
            ins = V.tensor_copy(gswd2[:, 512 * half:512 * half + 512], WO[half][:, 0:512])
        I(ins, sV)
        C.setms("swbe", sV.n)
        I(V.tensor_tensor(ln2s[:], gswd2[:], gbpb[:], ALU.mult), sV)
        C.setms("swm", sV.n)
        I(S.activation(scal1[0:1, 3:4], scal1[0:1, 1:2], AF.Sigmoid, bias=0.0, scale=1.0), sS)
        C.setms("sig2", sS.n)
        W(V, sS, M["sig2"])
        W(V, sCc, M["gm2"])
        V.tensor_tensor(scal1[0:1, 8:9], scal1[0:1, 3:4], gmax2[:], ALU.mult)
        vgap()
        vgap()
        V.reciprocal(scal1[0:1, 10:11], scal1[0:1, 8:9])
        V.tensor_scalar(scal1[0:1, 9:10], scal1[0:1, 8:9], 1.0 / 127.0, None, ALU.mult, ALU.bypass)
        vgap()
        I(V.tensor_scalar(scal1[0:1, 11:12], scal1[0:1, 10:11], 127.0, None,
                          ALU.mult, ALU.bypass), sV)
        C.setms("s2done", sV.n)
        W(T, sV, M["s2done"])
        W(T, sV, M["swbe"])
        T.matmul(WO[0][:, 0:1], ones_f[:], scal1[0:1, 11:12], start=True, stop=True)
        I(T.matmul(WO[1][:, 0:1], ones_f[:], scal1[0:1, 9:10], start=True, stop=True), sT)
        C.setms("s2bc", sT.n)
        W(V, sT, M["s2bc"])
        V.tensor_copy(is2col[:], WO[0][:, 0:1])
        I(V.tensor_copy(s2col[:], WO[1][:, 0:1]), sV)
        C.setms("s2col", sV.n)
        vgap()
        vgap()
        I(V.tensor_scalar(gswd2[:], ln2s[:], s2col[:], None, ALU.mult, ALU.bypass), sV)
        C.setms("gswd2", sV.n)
        bar3 = barrier("bar3")

        for gi in range(8):
            for ic in range(NIC):
                asl = ic % 4
                if ic >= 4:
                    W(Y, sS, M[f"rnd1_{gi}_{ic-4}"])
                elif gi > 0:
                    W(Y, sS, M[f"rnd1_{gi-1}_{NIC-4+ic}"])
                else:
                    W(Y, sB, bar3)
                    if gi == 0:
                        W(Y, sems["sAs0"], C.ms["as_tot0"])
                        W(Y, sems["sAs1"], C.ms["as_tot1"])
                I(Y.dma_start(out=arel[asl][:],
                              in_=asp[128 * ic:128 * (ic + 1), 512 * gi:512 * (gi + 1)]),
                  sAr[asl])
                C.setms(f"arld_{gi}_{ic}", (asl, sAr[asl].n))
                Wd(S, M[f"arld_{gi}_{ic}"], sAr)
                if gi == 0 and ic == 0:
                    W(S, sB, bar3)
                if ic >= 2:
                    W(S, sV, M[f"ra_{gi}_{ic-2}"])
                elif gi > 0:
                    W(S, sV, M[f"ra_{gi-1}_{NIC-2+ic}"])
                S.activation(sgb[ic % 2][:], arel[asl][:], AF.Identity,
                             bias=cmagP[:], scale=is2col[:])
                I(S.activation(sgb[ic % 2][:], sgb[ic % 2][:], AF.Identity,
                               bias=cmagN[:], scale=1.0), sS)
                C.setms(f"rnd1_{gi}_{ic}", sS.n)
                W(V, sS, M[f"rnd1_{gi}_{ic}"])
                if gi > 1 and ic == 0:
                    W(V, sT, M[f"dmmh_{4*(gi-2)+3}_1"])
                I(V.tensor_scalar(raG[gi % 2][:, ic * 512:(ic + 1) * 512], sgb[ic % 2][:],
                                  127.0, -128.0, ALU.min, ALU.max), sV)
                C.setms(f"ra_{gi}_{ic}", sV.n)
            rardy = M[f"ra_{gi}_{NIC-1}"]
            for st in range(4):
                tt = 4 * gi + st
                sl = st % 2
                if tt >= 2:
                    W(Y, sV, M[f"yevh_{tt-2}_1"])
                I(Y.dma_start(out=xr[1 + sl][:], in_=xsp[128 * tt:128 * (tt + 1), :]),
                  sXq[sl])
                C.setms(f"xr2ld_{tt}", (sl, sXq[sl].n))
                for half in range(2):
                    if half == 0:
                        W(T, sV, rardy)
                    if tt >= 1:
                        W(T, sV, M[f"yevh_{tt-1}_{half}"])
                    for ic in range(NIC):
                        ins = T.matmul(WO[half][:, 0:512],
                                       raG[gi % 2][:, ic * 512 + st * 128:
                                                    ic * 512 + st * 128 + 128],
                                       rwdT[:, ic * D + 512 * half:
                                            ic * D + 512 * half + 512],
                                       start=(ic == 0), stop=(ic == NIC - 1))
                    I(ins, sT)
                    C.setms(f"dmmh_{tt}_{half}", sT.n)
                    W(V, sT, M[f"dmmh_{tt}_{half}"])
                    if half == 0:
                        Wd(V, M[f"xr2ld_{tt}"], sXq)
                        if tt >= 2:
                            Wd(V, M[f"yw_{tt-2}"], sYw)
                    V.tensor_tensor(yst[tt % 2][:, 512 * half:512 * half + 512],
                                    WO[half][:, 0:512],
                                    gswd2[:, 512 * half:512 * half + 512], ALU.mult)
                    I(V.tensor_tensor(yst[tt % 2][:, 512 * half:512 * half + 512],
                                      yst[tt % 2][:, 512 * half:512 * half + 512],
                                      xr[1 + sl][:, 512 * half:512 * half + 512],
                                      ALU.add), sV)
                    C.setms(f"yevh_{tt}_{half}", sV.n)
                W(G, sV, M[f"yevh_{tt}_1"])
                I(G.dma_start(out=y_out[128 * tt:128 * (tt + 1), :], in_=yst[tt % 2][:]),
                  sYw[tt % 2])
                C.setms(f"yw_{tt}", (tt % 2, sYw[tt % 2].n))

        C.setms("yw_tot0", sYw[0].n)
        C.setms("yw_tot1", sYw[1].n)
        W(G, sems["sYw0"], M["yw_tot0"])
        W(G, sems["sYw1"], M["yw_tot1"])
        W(Y, sems["sYw0"], M["yw_tot0"])
        W(Y, sems["sYw1"], M["yw_tot1"])

    def mk_engs():
        return (EngProxy(C, "tensor", nc.tensor), EngProxy(C, "vector", nc.vector),
                EngProxy(C, "scalar", nc.scalar), EngProxy(C, "sync", nc.sync),
                EngProxy(C, "gpsimd", nc.gpsimd))

    def fresh_sems():
        return {n: Sem(h, 1 if n in ("sT", "sV", "sS", "sG", "sC", "sB") else 16)
                for n, h in sem_h.items()}

    C.active = "__dry__"
    T_, V_, S_, Y_, G_ = mk_engs()
    build_all(T_, V_, S_, Y_, G_, fresh_sems())
    C.first = False

    with nc.Block() as block:
        def emit(name):
            C.active = name
            T_, V_, S_, Y_, G_ = mk_engs()
            build_all(T_, V_, S_, Y_, G_, fresh_sems())

        @block.sync
        def _(eng):
            emit("sync")

        @block.gpsimd
        def _(eng):
            emit("gpsimd")

        @block.scalar
        def _(eng):
            emit("scalar")

        @block.vector
        def _(eng):
            emit("vector")

        @block.tensor
        def _(eng):
            emit("tensor")

    es.close()
    return nc


_CACHE = {}


def _host_prepare(inputs):
    f32 = np.float32
    x = np.asarray(inputs["L0_attn_x"], f32)
    wq = np.ascontiguousarray(np.asarray(inputs["L0_attn_wq"], f32))
    bq = np.asarray(inputs["L0_attn_bq"], f32)
    wk = np.ascontiguousarray(np.asarray(inputs["L0_attn_wk"], f32))
    bk = np.asarray(inputs["L0_attn_bk"], f32)
    wv = np.ascontiguousarray(np.asarray(inputs["L0_attn_wv"], f32))
    bv = np.asarray(inputs["L0_attn_bv"], f32)
    wo = np.ascontiguousarray(np.asarray(inputs["L0_attn_wo"], f32))
    bo = np.asarray(inputs["L0_attn_bo"], f32)
    wg = np.ascontiguousarray(np.asarray(inputs["L0_ffn_w_gate"], f32))
    wu = np.ascontiguousarray(np.asarray(inputs["L0_ffn_w_up"], f32))
    wd = np.ascontiguousarray(np.asarray(inputs["L0_ffn_w_down"], f32))
    clipx = np.asarray(inputs["L0_ffn_clip_x"], f32).reshape(1, 1)
    clipd = np.asarray(inputs["L0_ffn_clip_down"], f32).reshape(1, 1)
    c = np.asarray(inputs["L0_c"], f32)
    wmod = np.ascontiguousarray(np.asarray(inputs["L0_wmod"], f32))

    ident = np.eye(128, dtype=ml_dtypes.bfloat16)
    mlh = np.zeros((128, 256), np.float32)
    ish = np.zeros((64, 128), np.float32)
    for i in range(64):
        ish[i, 64 + i] = 1.0
    for b in (0, 64):
        mlh[b:b + 64, 0:128] = np.eye(128)[0:64]
        mlh[b:b + 64, 128:256] = ish
    band = np.zeros((128, 384), np.float32)
    for p in range(128):
        band[p, :p] = NEG_BIG
        band[p, p + 257:] = NEG_BIG
    mrh = np.zeros((128, 768), np.float32)
    for b in (0, 64):
        mrh[b:b + 64, 0:384] = band[0:64]
        mrh[b:b + 64, 384:768] = band[64:128]
    mlh = mlh.astype(ml_dtypes.bfloat16)
    mrh = mrh.astype(ml_dtypes.bfloat16)

    bqc = np.ascontiguousarray(bq.reshape(8, 128).T)
    bkc = np.ascontiguousarray(bk.reshape(8, 128).T)
    bvr = np.ascontiguousarray(bv.reshape(1, D))
    bor = np.ascontiguousarray(bo.reshape(1, D))

    in_maps = []
    for core in range(8):
        b = core // 2
        h = core % 2
        xs = np.zeros((NT * 128, D), np.float32)
        if h == 0:
            xs[128:NT * 128] = x[b, 0:(NT - 1) * 128]
            ef = np.array([0.0, 1.0], np.float32)
        else:
            xs[0:(NT - 1) * 128] = x[b, 8192 - (NT - 1) * 128:8192]
            ef = np.array([1.0, 0.0], np.float32)
        eflg = np.broadcast_to(ef, (128, 2)).copy()
        ccolb = np.ascontiguousarray(c[b].reshape(8, 128).T)
        in_maps.append({
            "x": xs, "wq": wq, "wk": wk, "wv": wv, "wo": wo,
            "bqc": bqc, "bkc": bkc, "bvr": bvr, "bor": bor,
            "wg": wg, "wu": wu, "wd": wd,
            "clipx": clipx, "clipd": clipd, "ccol": ccolb, "wmod": wmod,
            "eflg": eflg, "ident": ident, "identf": np.eye(128, dtype=np.float32),
            "mlh": mlh, "mrh": mrh,
        })
    return in_maps


def kernel(**inputs):
    if "nc" not in _CACHE:
        _CACHE["nc"] = build_program()
    nc = _CACHE["nc"]
    in_maps = _host_prepare(inputs)
    res = run_bass_kernel_spmd(nc, in_maps, core_ids=list(range(8)))
    out = np.zeros((4, 8192, D), np.float32)
    for core in range(8):
        b = core // 2
        h = core % 2
        out[b, h * 4096:(h + 1) * 4096] = res.results[core]["y"]
    return out
